# version 12
# speedup vs baseline: 1.6143x; 1.0626x over previous
import contextlib
import numpy as np
import concourse.bass as bass
import concourse.mybir as mybir
from concourse.bass_types import AP
from concourse.bass_utils import run_bass_kernel_spmd

F32 = mybir.dt.float32
BF16 = mybir.dt.bfloat16
AF = mybir.ActivationFunctionType
OP = mybir.AluOpType

HID = 2048
DIN = 4096
DST = 128
NH = 64
HD = 64
CONVD = DIN + 2 * DST
NSX = CONVD // 128
KTAP = 4
PROJ = 2 * (DIN + DST) + NH
CH = 128
EPS_LN = 1e-5
EPS_RMS = 1e-5
NCORE = 8


class Prog:

    QUEUES = {"gdma": "gpsimd", "adma": "scalar", "spdma": "sync",
              "sptx": "sync"}
    NQS = 12
    NQS_TX = 4
    BLOCKS = {"pe": "tensor", "act": "scalar", "dve": "vector",
              "pool": "gpsimd"}
    for _q, _b in QUEUES.items():
        for _j in range(NQS):
            BLOCKS[f"{_q}{_j}"] = _b

    def __init__(self, nc):
        self.nc = nc
        self.ops = []
        self.ctr = {k: 0 for k in self.BLOCKS}
        self.qrot = {q: 0 for q in self.QUEUES}
        self.sems = {}
        self.tx_extra = []

    def _op(self, semname, emit, deps, inc):
        self.ctr[semname] += inc
        self.ops.append((semname, emit,
                         tuple(t for t in deps if t is not None), inc))
        return (semname, self.ctr[semname])

    def _dma(self, q, emit, deps):
        j = self.qrot[q]
        n = self.NQS_TX if q == "sptx" else self.NQS
        self.qrot[q] = (j + 1) % n
        sn = f"{q}{j}"
        prev = (sn, self.ctr[sn]) if self.ctr[sn] > 0 else None
        return self._op(sn, emit, tuple(deps) + ((prev,) if prev else ()),
                        16)

    def pe(self, emit, deps=()):
        return self._op("pe", emit, deps, 1)

    def act(self, emit, deps=()):
        return self._op("act", emit, deps, 1)

    def dve(self, emit, deps=()):
        return self._op("dve", emit, deps, 1)

    def pool(self, emit, deps=()):
        return self._op("pool", emit, deps, 1)

    def gdma(self, emit, deps=()):
        return self._dma("spdma", emit, deps)

    def cdma(self, emit, deps=()):
        return self._dma("gdma", emit, deps)

    def adma(self, emit, deps=()):
        return self._dma("adma", emit, deps)

    def spdma(self, emit, deps=()):
        return self._dma("spdma", emit, deps)

    def sptx(self, emit, deps=()):
        return self._dma("sptx", emit,
                         tuple(deps) + tuple(self.tx_extra))

    def bar(self):
        return [(s, v) for s, v in self.ctr.items() if v > 0]

    def build(self, exits):
        nc = self.nc
        for s, v in self.ctr.items():
            if v > 0:
                self.sems[s] = exits.enter_context(nc.semaphore(s))
        per_block = {}
        for entry in self.ops:
            per_block.setdefault(self.BLOCKS[entry[0]], []).append(entry)
        final = {s: v for s, v in self.ctr.items() if v > 0}
        block = exits.enter_context(nc.Block())
        hw = {}

        def mk(bname, ops_list, is_final):
            def body(eng):
                h = hw.setdefault(bname, {})
                for semname, emit, deps, inc in ops_list:
                    for dsem, dval in deps:
                        if dsem == semname:
                            pass
                        if dval > 0 and h.get(dsem, 0) < dval and \
                                dsem in self.sems:
                            eng.wait_ge(self.sems[dsem], dval)
                            h[dsem] = dval
                    inst = emit(eng)
                    if inst is None:
                        raise RuntimeError(f"op on {semname} emitted nothing")
                    inst.then_inc(self.sems[semname], inc)
                    h[semname] = max(h.get(semname, 0), 0)
                if is_final:
                    for s, v in final.items():
                        if h.get(s, 0) < v:
                            eng.wait_ge(self.sems[s], v)
            return body

        names = list(per_block)
        if "sync" not in names:
            names.append("sync")
            per_block["sync"] = []
        for bname in names:
            getattr(block, bname)(mk(bname, per_block[bname],
                                     bname == "sync"))
        return block


class Arena:

    def __init__(self, nc, exits, words=49152, name="arena"):
        self.t = exits.enter_context(nc.sbuf_tensor(name, [128, words], F32))
        self.words = words
        self.off = 0
        self.marks = []
        self.peak = 0

    def f32(self, w):
        w = (w + 1) & ~1
        assert self.off + w <= self.words, \
            f"arena overflow: {self.off}+{w}>{self.words}"
        a = self.t[:, self.off:self.off + w]
        self.off += w
        self.peak = max(self.peak, self.off)
        return a

    def bf16(self, w):
        return self.f32((w + 1) // 2).bitcast(BF16)

    def push(self):
        self.marks.append(self.off)

    def pop(self):
        self.off = self.marks.pop()


def _as_ap(t):
    return t if isinstance(t, AP) else t.ap()


def strided(ap2d, off, step, count):
    ap2d = _as_ap(ap2d)
    return AP(ap2d.tensor, ap2d.offset + off,
              [list(ap2d.ap[0]), [step, count]])


def expand(ap2d, off, dims):
    ap2d = _as_ap(ap2d)
    return AP(ap2d.tensor, ap2d.offset + off,
              [list(ap2d.ap[0])] + [[s, c] for s, c in dims])


def it_lhsT(fmbuf, kslab, tb, fac):
    return strided(fmbuf, kslab * 128 * fac + tb, fac, 128)


def rep_dram(dram_t, offset, n, parts=128):
    a = dram_t.ap()
    return AP(a.tensor, a.offset + offset, [[0, parts], [1, n]])


def dram_tb(dram_t, blk, width):
    a = dram_t.ap()
    return AP(a.tensor, a.offset + blk * 128 * width,
              [[width, 128], [1, width]])


def build_program(TOK=4096, ncore=NCORE, dbg=False):
    T1 = 512
    NT1 = TOK // T1
    NTB = T1 // 128
    T2 = 1024 if TOK % 1024 == 0 else TOK
    NT2 = TOK // T2
    NTB2 = T2 // 128
    HTB = NTB2 // 2
    NCHUNK = TOK // CH

    nc = bass.Bass("TRN2", target_bir_lowering=False, debug=False)
    exits = contextlib.ExitStack()
    p = Prog(nc)

    def din(name, shape, dtype=F32):
        return nc.dram_tensor(name, shape, dtype, kind="ExternalInput")

    def dtmp(name, shape, dtype=F32):
        return nc.dram_tensor(name, shape, dtype)

    x_d = din("x", [TOK, HID])
    xT_d = din("xT", [HID, TOK])
    xhalo_d = din("xhalo", [3, HID])
    winT_d = din("winT", [HID, PROJ])
    woutT_d = din("woutT", [DIN, HID])
    w1T_d = din("w1T", [HID, 8192])
    w2T_d = din("w2T", [8192, HID])
    convw_d = din("conv_w", [CONVD, KTAP])
    convb_d = din("conv_b", [CONVD])
    dtb_d = din("dt_bias", [NH])
    alog_d = din("A_log", [NH])
    dvec_d = din("Dvec", [NH])
    normw_d = din("norm_w", [DIN])
    b1_d = din("b1", [8192])
    b2_d = din("b2", [HID])
    tril_d = din("tril", [128, 128])
    identf_d = din("identf", [128, 128])
    cmask_d = din("cmask", [1, 1])
    imask_d = din("imask", [1, 1])
    out_d = nc.dram_tensor("out", [TOK, HID], F32, kind="ExternalOutput")

    winT_bf = dtmp("winT_bf", [HID, PROJ], BF16)
    woutT_bf = dtmp("woutT_bf", [DIN, HID], BF16)
    w1qB_bf = dtmp("w1qB", [32, 128, 4096], BF16)
    w2qT_bf = dtmp("w2qT", [8192, HID], BF16)
    ln1x_sp = dtmp("ln1x_sp", [NT1, 128, 16 * T1], BF16)
    xbc_sp = dtmp("xbc_sp", [NT1, NSX, 128, T1], BF16)
    zsil_sp = dtmp("zsil_sp", [NT1, NTB, 128, DIN], BF16)
    acs_sp = dtmp("acs_sp", [NCHUNK, NH, CH], F32)
    chdecT_d = dtmp("chdecT", [NCHUNK, NH], F32)
    tdec_d = dtmp("tdec", [NT1, NH], F32)
    xa_sp = dtmp("xa_sp", [TOK, HID], F32)
    s_out_d = dtmp("s_out", [128, NH * HD], F32)
    s_in_d = dtmp("s_in", [128, NH * HD], F32)
    sc_d = dtmp("sc_d", [1, 2], F32)
    halob_d = dtmp("halob", [3, HID], F32)
    murow_d = dtmp("murow", [8, 128], F32)

    ar = Arena(nc, exits, words=40864)

    def sb(name, shape, dtype=F32):
        return exits.enter_context(nc.sbuf_tensor(name, shape, dtype))

    convw_s = sb("convw", [128, NSX * KTAP])
    convb_s = sb("convb", [128, NSX])
    dtb_s = sb("dtb", [64, 1])
    alog_s = sb("alog", [64, 1])
    negA_s = sb("negA", [128, 1])
    b1_s = sb("b1s", [128, 64])
    tril_s = sb("tril_s", [128, 128])
    identf_s = sb("identf_s", [128, 128])
    identb_s = sb("identb_s", [128, 128], BF16)
    cmcol_s = sb("cmcol", [128, 1])
    imcol_s = sb("imcol", [128, 1])
    onesf_s = sb("onesf", [128, 1])
    onesrow_s = sb("onesrow", [1, 128])
    inv1_s = sb("inv1", [128, 1])
    inv2_s = sb("inv2", [128, 1])
    qacc_s = sb("qacc", [128, 256])
    stat6 = sb("stat6", [128, 8, 6])
    stat2 = sb("stat2", [128, 8, 2])
    rs_s = sb("rs", [128, 8, 1])
    mrs_s = sb("mrs", [128, 8, 1])
    tiny = sb("tiny", [128, 16])
    cst_s = sb("cst", [128, 2])
    tiny2 = sb("tiny2", [64, 128])
    acsdt_s = sb("acsdt", [128, TOK])
    acse_s = sb("acse", [64, NCHUNK])
    state_s = sb("state", [128, NH * HD])
    prevb_s = sb("prevb", [128, 1024], BF16)
    chd_s = sb("chd", [128, 64])
    qin_s = sb("qin", [128, 1024])
    qt1_s = sb("qt1", [128, 1024])
    qq_s = sb("qq", [128, 512], BF16)
    acs_v = acsdt_s[0:64, :]
    dt_v = acsdt_s[64:128, :]

    pA = exits.enter_context(nc.psum_tensor("pA", [128, 4096], F32))

    def bank(i, w=512):
        return pA[:, i * 512:i * 512 + w]

    pbank6b = pA[:, 6 * 512:7 * 512].bitcast(BF16)
    pbank7b = pA[:, 7 * 512:8 * 512].bitcast(BF16)

    def pe_tx(out_blocks, in_blocks, deps, banks=(6, 7)):
        pb = {6: pbank6b, 7: pbank7b}
        toks = []
        evac_by_bank = {}
        bi = 0
        i = 0
        first = True
        while i < len(in_blocks):
            n = min(8, len(in_blocks) - i)
            bank_ = banks[bi % len(banks)]
            bi += 1
            buf = pb[bank_]
            mmt = None
            for j in range(n):
                dps = [mmt] if mmt is not None else (
                    list(deps) + [t_idb] +
                    ([evac_by_bank[bank_]] if bank_ in evac_by_bank else []))
                if first and j == 0:
                    dps = list(deps) + [t_idb]
                mmt = p.pe(lambda e, j=j, i=i, buf=buf: e.transpose(
                    buf[:, j * 128:(j + 1) * 128], in_blocks[i + j],
                    identb_s[:, :]), deps=dps)
            first = False
            ev = None
            for j in range(n):
                ev = p.dve(lambda e, j=j, i=i, buf=buf: e.tensor_copy(
                    out_blocks[i + j], buf[:, j * 128:(j + 1) * 128]),
                    deps=[mmt] + ([ev] if ev else toks[-1:]))
                toks.append(ev)
            evac_by_bank[bank_] = ev
            i += n
        return toks

    def repmm(psum_ap, row_ap, deps):
        return p.pe(lambda e: e.matmul(psum_ap, onesrow_s[:, :], row_ap,
                                       start=True, stop=True),
                    deps=list(deps) + [t_ones2])

    t_convw = p.gdma(lambda e: e.dma_start(
        out=expand(convw_s, 0, [(KTAP, NSX), (1, KTAP)]),
        in_=convw_d.ap().rearrange("(s p) k -> p s k", p=128)))
    t_convb = p.gdma(lambda e: e.dma_start(
        out=convb_s[:, :], in_=convb_d.ap().rearrange("(s p) -> p s", p=128),
        allow_slow_non_contiguous=True))
    t_dtb = p.gdma(lambda e: e.dma_start(
        out=dtb_s[:, :], in_=dtb_d.ap().rearrange("(a b) -> a b", b=1)))
    t_alog = p.gdma(lambda e: e.dma_start(
        out=alog_s[:, :], in_=alog_d.ap().rearrange("(a b) -> a b", b=1)))
    t_b1 = p.gdma(lambda e: e.dma_start(
        out=b1_s[:, :], in_=b1_d.ap().rearrange("(s p) -> p s", p=128),
        allow_slow_non_contiguous=True))
    t_tril = p.gdma(lambda e: e.dma_start(out=tril_s[:, :], in_=tril_d[:, :]))
    t_idf = p.gdma(lambda e: e.dma_start(out=identf_s[:, :],
                                         in_=identf_d[:, :]))
    t_idb = p.dve(lambda e: e.tensor_copy(identb_s[:, :], identf_s[:, :]),
                  deps=[t_idf])
    t_cm0 = p.gdma(lambda e: e.dma_start(out=tiny[0:1, 12:13],
                                         in_=cmask_d[:, :]))
    t_im0 = p.gdma(lambda e: e.dma_start(out=tiny[0:1, 13:14],
                                         in_=imask_d[:, :]))
    t_negA = p.act(lambda e: e.activation(negA_s[64:128, :], alog_s[:, :],
                                          AF.Exp), deps=[t_alog])
    t_negA = p.dve(lambda e: e.tensor_scalar_mul(negA_s[64:128, :],
                                                 negA_s[64:128, :],
                                                 -1.0), deps=[t_negA])
    t_ones = p.dve(lambda e: e.memset(onesf_s[:, :], 1.0))
    t_ones2 = p.dve(lambda e: e.memset(onesrow_s[:, :], 1.0))
    t_st0 = p.dve(lambda e: e.memset(state_s[:, :], 0.0))
    t_qa0 = p.dve(lambda e: e.memset(qacc_s[:, :], 0.0))
    t_cmm = repmm(bank(6, 2), tiny[0:1, 12:14], [t_cm0, t_im0])
    t_cm = p.dve(lambda e: e.tensor_copy(cmcol_s[:, :], bank(6, 2)[:, 0:1]),
                 deps=[t_cmm])
    t_im = p.dve(lambda e: e.tensor_copy(imcol_s[:, :], bank(6, 2)[:, 1:2]),
                 deps=[t_cmm, t_cm])
    t_c0 = p.dve(lambda e: e.memset(cst_s[:, 0:1], EPS_LN))
    t_c1 = p.dve(lambda e: e.memset(cst_s[:, 1:2], 1.0))
    t_cst = [t_c0, t_c1]

    cast_toks = []
    for (src, dst, rows) in ((winT_d, winT_bf, HID), (woutT_d, woutT_bf, DIN)):
        for a0 in range(0, rows, 512):
            b0 = min(rows, a0 + 512)
            cast_toks.append(p.cdma(
                lambda e, s=src, d=dst, a=a0, b=b0:
                e.dma_start(out=d[a:b, :], in_=s[a:b, :])))
    t_cast_win = cast_toks[:HID // 512]
    t_cast_wout = cast_toks[HID // 512:]

    def rsqrt_col(dst, var_col, eps, deps):
        nparts = var_col.shape[0]
        tk = p.act(lambda e: e.activation(dst, var_col, AF.Ln,
                                          bias=cst_s[0:nparts, 0:1]),
                   deps=deps + t_cst)
        return p.act(lambda e: e.activation(dst, dst, AF.Exp, scale=-0.5),
                     deps=[tk])

    def ln_stats(x_ap, tb, deps):
        x_ap = _as_ap(x_ap)
        W = x_ap.shape[-1]
        nparts = x_ap.shape[0]
        nsl = (W + 511) // 512
        ts1 = []
        for j in range(nsl):
            w0 = j * 512
            w1 = min(W, w0 + 512)
            sl = AP(x_ap.tensor, x_ap.offset + w0,
                    [list(x_ap.ap[0]), [1, w1 - w0]])
            ts1.append(p.dve(
                lambda e, sl=sl, j=j: e.bn_stats(
                    stat6[0:nparts, j, :], sl), deps=deps + ts1[-1:]))
        t2 = p.dve(lambda e: e.bn_aggr(
            stat2[0:nparts, tb, :],
            expand(stat6, 0, [(6, nsl), (1, 6)])[0:nparts]), deps=ts1)
        t3 = rsqrt_col(rs_s[0:nparts, tb, :], stat2[0:nparts, tb, 1:2],
                       EPS_LN, [t2])
        t4 = p.dve(lambda e: e.tensor_mul(mrs_s[0:nparts, tb, :],
                                          stat2[0:nparts, tb, 0:1],
                                          rs_s[0:nparts, tb, :]),
                   deps=[t3, t2])
        return t4

    scale_jobs = []
    acc_i = 0
    for (wd, nr, ncol) in ((w1T_d, HID, 8192), (w2T_d, 8192, HID)):
        for r in range(0, nr, 128):
            for c0 in range(0, ncol, 1024):
                scale_jobs.append((wd, r, c0, acc_i))
                acc_i += 1
    N1CNT = (HID // 128) * 8
    N2CNT = (8192 // 128) * 2
    assert acc_i == N1CNT + N2CNT <= 256
    quant_jobs = []
    for (wd, qd, nr, ncol) in ((w1T_d, w1qB_bf, HID, 8192),
                               (w2T_d, w2qT_bf, 8192, HID)):
        for r in range(0, nr, 128):
            for c0 in range(0, ncol, 1024):
                quant_jobs.append((wd, qd, r, c0))

    def qstore_ap(qd, r, c0):
        if qd is w1qB_bf:
            a = qd.ap()
            off = a.offset + (c0 // 256) * (128 * 4096) + (r // 128) * 256
            return AP(a.tensor, off, [[4096, 128], [128 * 4096, 2], [1, 256]])
        return qd[r:r + 128, c0:c0 + 512]
    qstate = dict(si=0, stoks=[], inv=None, qi=0, qtoks=[], qrd=[])

    def emit_scale(n):
        while n > 0 and qstate["si"] < len(scale_jobs):
            wd, r, c0, acc = scale_jobs[qstate["si"]]
            buf = qin_s if qstate["si"] % 2 == 0 else qt1_s
            war = qstate["stoks"][-2:-1]
            tl = p.cdma(
                lambda e, wd=wd, r=r, c0=c0, buf=buf:
                e.dma_start(out=buf[:, :],
                            in_=wd[r:r + 128, c0:c0 + 1024]),
                deps=war)
            tr = p.dve(lambda e, acc=acc, buf=buf:
                       e.tensor_reduce(qacc_s[:, acc:acc + 1], buf[:, :],
                                       mybir.AxisListType.X, OP.add,
                                       apply_absolute_value=True),
                       deps=[tl, t_qa0])
            qstate["stoks"].append(tr)
            qstate["si"] += 1
            n -= 1

    def finish_scale(extra=()):
        allst = [qstate["stoks"][-1]] + list(extra)
        tr1 = p.dve(lambda e: e.reduce_sum(tiny[:, 0:1], qacc_s[:, 0:N1CNT],
                                           mybir.AxisListType.X), deps=allst)
        tr2 = p.dve(lambda e: e.reduce_sum(
            tiny[:, 1:2], qacc_s[:, N1CNT:N1CNT + N2CNT],
            mybir.AxisListType.X), deps=allst + [tr1])
        tmm = p.pe(lambda e: e.matmul(bank(7, 2)[0:1, :], onesf_s[:, :],
                                      tiny[:, 0:2], start=True, stop=True),
                   deps=[tr1, tr2, t_ones] + list(extra))
        tm = p.dve(lambda e: e.tensor_scalar(
            tiny[0:1, 2:4], bank(7, 2)[0:1, :], 1.0 / (HID * 8192.0), 1e-8,
            OP.mult, OP.max), deps=[tmm])
        trc = p.dve(lambda e: e.reciprocal(tiny[0:1, 4:6], tiny[0:1, 2:4]),
                    deps=[tm])
        tmm2 = repmm(bank(7, 2), tiny[0:1, 4:6], [trc, tmm])
        tev = p.dve(lambda e: e.tensor_copy(inv1_s[:, :],
                                            bank(7, 2)[:, 0:1]),
                    deps=[tmm2])
        tev2 = p.dve(lambda e: e.tensor_copy(inv2_s[:, :],
                                             bank(7, 2)[:, 1:2]),
                     deps=[tmm2, tev])
        qstate["fs_dve"] = tev2
        qstate["inv"] = [tev, tev2]

    def emit_quant(n):
        while n > 0 and qstate["qi"] < len(quant_jobs):
            wd, qd, r, c0 = quant_jobs[qstate["qi"]]
            inv = inv1_s if wd is w1T_d else inv2_s
            buf = qin_s
            war = qstate["qrd"][-1:]
            tl = p.cdma(
                lambda e, wd=wd, r=r, c0=c0, buf=buf:
                e.dma_start(out=buf[:, :],
                            in_=wd[r:r + 128, c0:c0 + 1024]),
                deps=war + qstate["inv"])
            t1 = p.dve(lambda e, inv=inv, buf=buf: e.tensor_scalar(
                qt1_s[:, :], buf[:, :], inv[:, :], 0.5, OP.mult, OP.is_ge),
                deps=[tl] + qstate["qrd"][-1:] + qstate["stoks"][-1:])
            t2 = p.dve(lambda e, inv=inv, buf=buf: e.tensor_scalar(
                buf[:, :], buf[:, :], inv[:, :], -0.5, OP.mult, OP.is_le),
                deps=[tl, t1])
            t3 = p.dve(lambda e, buf=buf: e.tensor_sub(
                qq_s[:, :], qt1_s[:, 0:512], buf[:, 0:512]),
                deps=[t1, t2] + qstate["qtoks"][-2:-1])
            ts_ = p.gdma(lambda e, qd=qd, r=r, c0=c0:
                         e.dma_start(out=qstore_ap(qd, r, c0),
                                     in_=qq_s[:, :]), deps=[t3])
            qq2_s = qacc_s[:, 0:256].bitcast(BF16)
            t4 = p.dve(lambda e, buf=buf, qq2_s=qq2_s: e.tensor_sub(
                qq2_s[:, :], qt1_s[:, 512:1024], buf[:, 512:1024]),
                deps=[t1, t2, t3] + qstate["qtoks"][-1:] + qstate["inv"])
            ts2 = p.gdma(lambda e, qd=qd, r=r, c0=c0:
                         e.dma_start(out=qstore_ap(qd, r, c0 + 512),
                                     in_=qq2_s[:, :]), deps=[t4])
            qstate["qrd"].append(t4)
            qstate["qtoks"].append(ts_)
            qstate["qtoks"].append(ts2)
            qstate["qi"] += 1
            n -= 1

    ar.push()
    x1_s = ar.f32(HID)
    xTf_s = ar.f32(16 * T1)
    tB = ar.f32(T1)
    mu8_s = ar.f32(8)
    mur_s = ar.f32(128)
    ln1f = ar.bf16(16 * T1)
    wog = [ar.bf16(16 * 256) for _ in range(2)]
    xbcr_s = ar.f32(515 + 3 * NSX)
    xbc_f = ar.bf16(NSX * T1)
    xssd_tm = ar.bf16(32 * 128 * NTB)
    xdd_tm = ar.bf16(NH * HD)
    btm_s = ar.bf16(128 * NTB)
    dtraw_s = ar.f32(T1)
    cacc_s = ar.f32(T1)
    tA = cacc_s
    ddf_s = ar.f32(CH)
    ddtm_s = ar.f32(NTB * 64)
    tdcol_s = ar.f32(64)
    gsum_s = ar.f32(NTB + 1)
    halo_s = xTf_s
    halof_s = ar.bf16(16 * 4)
    assert ar.off <= ar.words, ar.off

    t_xh = p.gdma(lambda e: e.dma_start(out=halo_s[0:3, :HID],
                                        in_=xhalo_d[:, :]))
    th2 = ln_stats(halo_s[0:3, :HID], 0, [t_xh])
    th3 = th2
    th4 = p.dve(lambda e: e.tensor_scalar(
        halo_s[0:3, :HID], halo_s[0:3, :HID], stat2[0:3, 0, 0:1],
        rs_s[0:3, 0, :], OP.subtract, OP.mult), deps=[th3, t_xh])
    th5 = p.gdma(lambda e: e.dma_start(out=halob_d[:, :],
                                       in_=halo_s[0:3, :HID]), deps=[th4])
    _hz = p.dve(lambda e: e.memset(halof_s[:, :], 0.0))
    _hfm = [_hz]
    for s_ in range(16):
        _hfm.append(p.cdma(lambda e, s_=s_: e.dma_start(
            out=halof_s[:, s_ * 4:s_ * 4 + 3],
            in_=halob_d[:, s_ * 128:(s_ + 1) * 128].rearrange("t p -> p t"),
            allow_slow_non_contiguous=True), deps=[th5, _hz]))
    halo_fm_tok = None

    p.tx_extra = cast_toks + _hfm
    OG = [(o, 256) for o in range(0, 4352, 256)] + [(4352, 64)]
    conv_done = [None] * NSX
    carry_tok = [None] * NSX
    state_tok = t_st0
    sp_acs_tok = None
    tile_end_bar = []
    wog_tok = [None, None]
    last_conv_dve = None
    last_silu = None
    prev_tfm = []
    prev_gm2 = None
    prev_td4m = None

    for ti in range(NT1):
        tok0 = ti * T1
        bar0 = []
        t_xT = p.gdma(lambda e, ti=ti: e.dma_start(
            out=expand(xTf_s, 0, [(T1, 16), (1, T1)]),
            in_=xT_d.ap().rearrange("(s q) t -> q s t", q=128)[
                :, :, ti * T1:(ti + 1) * T1]),
            deps=prev_tfm[-1:] + ([th5] if ti == 0 else []))
        lnt = []
        for tb in range(NTB):
            xload = p.gdma(lambda e, ti=ti, tb=tb: e.dma_start(
                out=x1_s, in_=dram_tb(x_d, ti * NTB + tb, HID)),
                deps=prev_tfm[-1:] + lnt[-1:] +
                ([prev_td4m] if tb == 0 and prev_td4m else []))
            st = ln_stats(x1_s, tb, [xload] +
                          ([prev_gm2] if tb == 0 and prev_gm2 else []))
            lnt.append(st)
        mrow_s = x1_s[0:1, 1024:2048]
        gm1 = p.dve(lambda e: e.tensor_copy(
            mu8_s[:, 0:NTB], expand(stat2, 0, [(2, NTB)])), deps=lnt)
        gm2 = p.dve(lambda e: e.tensor_copy(
            mu8_s[:, NTB:2 * NTB], expand(rs_s, 0, [(1, NTB)])),
            deps=lnt + [gm1])
        gm3 = p.pe(lambda e: e.transpose(bank(7, 128)[0:8, :],
                                         mu8_s[:, 0:8], identf_s[:, :]),
                   deps=[gm1, gm2, t_idf])
        gm4 = p.dve(lambda e: e.tensor_copy(mur_s[0:8, 0:128],
                                            bank(7, 128)[0:8, :]),
                    deps=[gm3])
        gm5 = p.gdma(lambda e: e.dma_start(out=murow_d[:, :],
                                           in_=mur_s[0:8, 0:128]),
                     deps=[gm4])
        gm5b = p.gdma(lambda e: e.dma_start(
            out=mrow_s[:, :],
            in_=AP(murow_d.ap().tensor, 0, [[1024, 1], [1, 1024]])),
            deps=[gm5])
        gm6 = repmm(bank(4), mrow_s[0:1, 0:T1], [gm5b, lnt[-1]])
        gm7 = repmm(bank(5), mrow_s[0:1, T1:2 * T1], [gm5b, gm6])
        lnsc_v = x1_s[:, 0:T1]
        tfm = []
        for s in range(16):
            a1t = p.dve(lambda e, s=s: e.tensor_sub(
                lnsc_v, xTf_s[:, s * T1:(s + 1) * T1], bank(4)),
                deps=[t_xT, gm6, gm2] + tfm[-1:])
            a2t = p.dve(lambda e, s=s: e.tensor_mul(
                ln1f[:, s * T1:(s + 1) * T1], lnsc_v, bank(5)),
                deps=[a1t, gm7])
            tfm.append(a2t)
        t_lnsp = p.gdma(lambda e, ti=ti: e.dma_start(
            out=ln1x_sp[ti], in_=ln1f[:, :]), deps=tfm)
        dt_tok = None
        bank_reader = {}
        for gi, (o0, ow) in enumerate(OG):
            par = gi % 2

            def load_wog(e, o0=o0, ow=ow, par=par):
                a = winT_bf.ap()
                src = AP(a.tensor, a.offset + DIN + o0,
                         [[PROJ, 128], [128 * PROJ, 16], [1, ow]])
                return e.dma_start(
                    out=expand(wog[par], 0, [(ow, 16), (1, ow)]), in_=src)
            t_w = p.gdma(load_wog,
                         deps=([wog_tok[par]] if wog_tok[par] else []) +
                         (t_cast_win if ti == 0 else []))
            nslabs = (ow + 127) // 128
            mm_tok = None
            grp_pe = None
            for si in range(nslabs):
                oslab = (o0 + si * 128) // 128
                sw = min(128, ow - si * 128)
                bk = (gi % 2) * 2 + si

                def mms(e, si=si, bk=bk, par=par, ow=ow, sw=sw):
                    r = None
                    for k in range(16):
                        r = e.matmul(
                            bank(bk)[0:sw, :] if sw < 128 else bank(bk),
                            wog[par][:, k * ow + si * 128:
                                     k * ow + si * 128 + sw],
                            ln1f[:, k * T1:(k + 1) * T1],
                            start=(k == 0), stop=(k == 15))
                    return r
                mm_tok = p.pe(mms, deps=tfm + [t_w] +
                              ([bank_reader[bk]] if bk in bank_reader
                               else []))
                grp_pe = mm_tok
                if oslab < NSX:
                    tE = p.dve(lambda e, bk=bk:
                               e.tensor_copy(xbcr_s[:, 3:3 + T1], bank(bk)),
                               deps=[mm_tok] +
                               ([last_conv_dve] if last_conv_dve else []))
                    bank_reader[bk] = tE
                    if ti == 0:
                        def hmm(e, si=si, par=par, ow=ow, sw=sw):
                            r = None
                            for k in range(16):
                                r = e.matmul(
                                    bank(7, 3)[0:sw, :] if sw < 128
                                    else bank(7, 3),
                                    wog[par][:, k * ow + si * 128:
                                             k * ow + si * 128 + sw],
                                    halof_s[:, k * 4:k * 4 + 3],
                                    start=(k == 0), stop=(k == 15))
                            return r
                        thm = p.pe(hmm, deps=_hfm + [t_w, mm_tok] +
                                   ([last_conv_dve] if last_conv_dve
                                    else []))
                        grp_pe = thm
                        tcar = p.dve(lambda e: e.tensor_copy(
                            xbcr_s[:, 0:3], bank(7, 3)), deps=[thm, tE])
                    else:
                        tcar = p.dve(lambda e, oslab=oslab: e.tensor_copy(
                            xbcr_s[:, 0:3],
                            xbcr_s[:, 515 + oslab * 3:515 + oslab * 3 + 3]),
                            deps=[carry_tok[oslab], tE])
                    tc = p.dve(lambda e, oslab=oslab: e.tensor_scalar_mul(
                        cacc_s[:, :], xbcr_s[:, 0:T1],
                        convw_s[:, oslab * KTAP:oslab * KTAP + 1]),
                        deps=[tE, tcar, t_convw] +
                        ([last_silu] if last_silu else []))
                    for tap in range(1, KTAP):
                        tc = p.dve(lambda e, oslab=oslab, tap=tap:
                                   e.scalar_tensor_tensor(
                                       cacc_s[:, :],
                                       xbcr_s[:, tap:tap + T1],
                                       convw_s[:, oslab * KTAP + tap:
                                               oslab * KTAP + tap + 1],
                                       cacc_s[:, :], OP.mult, OP.add),
                                   deps=[tc])
                    tnc = p.dve(lambda e, oslab=oslab: e.tensor_copy(
                        xbcr_s[:, 515 + oslab * 3:515 + oslab * 3 + 3],
                        xbcr_s[:, 512:515]), deps=[tc])
                    carry_tok[oslab] = tnc
                    last_conv_dve = tnc
                    tsl = p.act(lambda e, oslab=oslab: e.activation(
                        xbc_f[:, oslab * T1:(oslab + 1) * T1], cacc_s[:, :],
                        AF.Silu, bias=convb_s[:, oslab:oslab + 1]),
                        deps=[tc, t_convb])
                    conv_done[oslab] = tsl
                    last_silu = tsl
                else:
                    tE = p.dve(lambda e, bk=bk:
                               e.tensor_copy(dtraw_s[0:64, :],
                                             bank(bk)[0:64, :]),
                               deps=[mm_tok] +
                               ([last_conv_dve] if last_conv_dve else []))
                    bank_reader[bk] = tE
                    dt_tok = tE
            wog_tok[par] = grp_pe
        t_xbcsp = p.gdma(lambda e, ti=ti: e.dma_start(
            out=xbc_sp[ti].rearrange("s q f -> q s f"),
            in_=expand(xbc_f, 0, [(T1, NSX), (1, T1)])),
            deps=[conv_done[s] for s in range(NSX)])
        d0 = dtraw_s[0:64, :]
        c1 = p.act(lambda e: e.activation(tA[0:64, :], d0, AF.Abs,
                                          bias=dtb_s[:, :]),
                   deps=[dt_tok, t_dtb, last_silu])
        c2 = p.act(lambda e: e.activation(tA[0:64, :], tA[0:64, :], AF.Exp,
                                          scale=-1.0), deps=[c1])
        c3 = p.act(lambda e: e.activation(tA[0:64, :], tA[0:64, :], AF.Ln,
                                          bias=cst_s[0:64, 1:2]),
                   deps=[c2] + t_cst)
        c4 = p.act(lambda e: e.activation(tB[0:64, :], d0, AF.Relu,
                                          bias=dtb_s[:, :]),
                   deps=[dt_tok, t_dtb, c3, tfm[-1]])
        c5 = p.dve(lambda e, tok0=tok0: e.tensor_add(
            dt_v[:, tok0:tok0 + T1], tA[0:64, :], tB[0:64, :]),
            deps=[c3, c4])
        c7 = p.dve(lambda e, tok0=tok0: e.tensor_scalar_mul(
            tB[64:128, :], dt_v[:, tok0:tok0 + T1], negA_s[64:128, :]),
            deps=[c5, t_negA, c4])
        acs_toks = []
        for cc in range(NTB):
            tk = p.dve(lambda e, tok0=tok0, cc=cc: e.tensor_tensor_scan(
                acs_v[:, tok0 + cc * CH:tok0 + (cc + 1) * CH],
                tB[64:128, cc * CH:(cc + 1) * CH],
                tB[64:128, cc * CH:(cc + 1) * CH], 0.0, OP.add, OP.bypass),
                deps=[c7])
            acs_toks.append(tk)
        c9 = p.dve(lambda e, ti=ti, tok0=tok0: e.tensor_copy(
            acse_s[:, ti * NTB:(ti + 1) * NTB],
            strided(acs_v, tok0 + CH - 1, CH, NTB)), deps=acs_toks)
        sp_acs_tok = p.gdma(lambda e, ti=ti, tok0=tok0: e.dma_start(
            out=acs_sp.ap().rearrange("c h l -> h c l")[:, ti * NTB:
                                                        (ti + 1) * NTB, :],
            in_=expand(acs_v, tok0, [(CH, NTB), (1, CH)])[0:64]),
            deps=acs_toks)
        g1 = p.dve(lambda e: e.memset(gsum_s[0:64, NTB - 1:NTB], 0.0),
                   deps=[c9])
        gtoks = [g1]
        for cc in range(NTB - 2, -1, -1):
            gk = p.dve(lambda e, cc=cc, ti=ti: e.tensor_add(
                gsum_s[0:64, cc:cc + 1], gsum_s[0:64, cc + 1:cc + 2],
                acse_s[:, ti * NTB + cc + 1:ti * NTB + cc + 2]),
                deps=[gtoks[-1], c9])
            gtoks.append(gk)
        td1 = p.dve(lambda e, ti=ti: e.tensor_add(
            gsum_s[0:64, NTB:NTB + 1], gsum_s[0:64, 0:1],
            acse_s[:, ti * NTB:ti * NTB + 1]), deps=[gtoks[-1], c9])
        td2 = p.act(lambda e: e.activation(gsum_s[0:64, NTB:NTB + 1],
                                           gsum_s[0:64, NTB:NTB + 1],
                                           AF.Exp), deps=[td1])
        td3 = p.gdma(lambda e, ti=ti: e.dma_start(
            out=tdec_d[ti:ti + 1, :],
            in_=gsum_s[0:64, NTB:NTB + 1]), deps=[td2])
        td3b = p.gdma(lambda e, ti=ti: e.dma_start(
            out=mrow_s[0:1, 0:64], in_=tdec_d[ti:ti + 1, :]),
            deps=[td3, tfm[-1], gm6])
        td4m = repmm(bank(7, 64), mrow_s[0:1, 0:64],
                     [td3b] + dd_toks[-1:] if False else [td3b])
        td4 = p.dve(lambda e: e.tensor_copy(tdcol_s[:, :], bank(7, 64)),
                    deps=[td4m])
        inb = [xbc_f[:, s * T1 + tb * 128:s * T1 + tb * 128 + 128]
               for s in range(32) for tb in range(NTB)]
        outb = [xssd_tm[:, (s * NTB + tb) * 128:(s * NTB + tb) * 128 + 128]
                for s in range(32) for tb in range(NTB)]
        txp = pe_tx(outb, inb, [conv_done[s] for s in range(32)] +
                    [td4, td4m])
        inb = [xbc_f[:, 32 * T1 + tb * 128:32 * T1 + tb * 128 + 128]
               for tb in range(NTB)]
        outb = [btm_s[:, tb * 128:(tb + 1) * 128] for tb in range(NTB)]
        tbt_l = pe_tx(outb, inb, [conv_done[32]] + txp[-1:])
        tbt = tbt_l[-1]
        dd_toks = []
        prev_d4 = None
        for cc in range(NTB):
            tok_c = tok0 + cc * CH
            b1t = p.dve(lambda e, cc=cc, ti=ti: e.tensor_add(
                tiny2[:, 4:5], acse_s[:, ti * NTB + cc:ti * NTB + cc + 1],
                gsum_s[0:64, cc:cc + 1]), deps=gtoks + [c9])
            d1 = p.act(lambda e, tok_c=tok_c: e.activation(
                ddf_s[64:128, :], acs_v[:, tok_c:tok_c + CH], AF.Exp,
                scale=-1.0, bias=tiny2[:, 4:5]), deps=[b1t] + acs_toks)
            d2 = p.dve(lambda e, tok_c=tok_c: e.tensor_mul(
                ddf_s[64:128, :], ddf_s[64:128, :],
                dt_v[:, tok_c:tok_c + CH]), deps=[d1, c5])
            d3 = p.pe(lambda e: e.transpose(bank(7, 64), ddf_s[64:128, :],
                                            identf_s[64:128, 64:128]),
                      deps=[d2, t_idf] + tbt_l[-1:] +
                      ([prev_d4] if prev_d4 else []))
            d4 = p.dve(lambda e, cc=cc: e.tensor_copy(
                ddtm_s[:, cc * 64:(cc + 1) * 64], bank(7, 64)), deps=[d3])
            prev_d4 = d4
            dd_toks.append(d4)
        cs_mm_toks = []
        for cc in range(NTB):
            x5 = p.dve(lambda e, cc=cc: e.tensor_tensor(
                expand(xdd_tm, 0, [(64, 64), (1, 64)]),
                expand(xssd_tm, cc * 128, [(NTB * 128, 32), (1, 128)]),
                expand(ddtm_s, cc * 64, [(2, 32), (1, 2), (0, 64)]),
                OP.mult), deps=[dd_toks[cc]] + txp + cs_mm_toks[-1:])

            def csmm(e, cc=cc):
                r = None
                for h in range(NH):
                    r = e.matmul(
                        pA[:, h * 64:(h + 1) * 64],
                        btm_s[:, cc * 128:(cc + 1) * 128],
                        xdd_tm[:, h * 64:(h + 1) * 64],
                        start=(cc == 0 and h % 8 == 0),
                        stop=(cc == NTB - 1 and h % 8 == 7))
                return r
            mm = p.pe(csmm, deps=[x5] + tbt_l)
            cs_mm_toks.append(mm)
        s1 = p.dve(lambda e: e.tensor_tensor(
            state_s[:, :], state_s[:, :],
            expand(tdcol_s, 0, [(1, 64), (0, 64)]), OP.mult),
            deps=[state_tok, td4])
        s2 = p.dve(lambda e: e.tensor_add(state_s[:, :], state_s[:, :],
                                          pA[:, :]),
                   deps=[s1] + cs_mm_toks)
        state_tok = s2
        emit_scale(32)
        if qstate["si"] >= len(scale_jobs) and qstate["inv"] is None:
            finish_scale([state_tok])
        if qstate["inv"] is not None:
            emit_quant(24)
        prev_tfm = tfm
        prev_gm2 = gm2
        prev_td4m = td4m
        tile_end_bar.append([state_tok, t_xbcsp, t_lnsp, sp_acs_tok,
                             cs_mm_toks[-1], tbt, conv_done[NSX - 1]] + txp)

    cm1 = p.dve(lambda e: e.tensor_scalar_mul(state_s[:, :], state_s[:, :],
                                              cmcol_s[:, :]),
                deps=[state_tok, t_cm])
    cw = p.gdma(lambda e: e.dma_start(out=s_out_d[:, :], in_=state_s[:, :]),
                deps=[cm1])
    ccp = p.pool(lambda e: e.collective_compute(
        "AllReduce", OP.add,
        replica_groups=[[2 * i, 2 * i + 1] for i in range(ncore // 2)],
        ins=[s_out_d.ap().opt()], outs=[s_in_d.ap().opt()]), deps=[cw])
    cr = p.gdma(lambda e: e.dma_start(out=state_s[:, :], in_=s_in_d[:, :]),
                deps=[ccp, cm1])
    cm2 = p.dve(lambda e: e.tensor_scalar_mul(state_s[:, :], state_s[:, :],
                                              imcol_s[:, :]),
                deps=[cr, t_im])
    state_tok = cm2

    while qstate["si"] < len(scale_jobs):
        emit_scale(16)
    if qstate["inv"] is None:
        finish_scale([state_tok])

    assert NCHUNK <= 128
    e1 = p.act(lambda e: e.activation(tiny2[:, 0:NCHUNK], acse_s[:, :],
                                      AF.Exp), deps=[state_tok])
    e2 = p.pe(lambda e: e.transpose(bank(7, 64)[0:NCHUNK, :],
                                    tiny2[:, 0:NCHUNK],
                                    identf_s[0:64, 0:64]),
              deps=[e1, t_idf, qstate.get("fs_dve")])
    e3 = p.dve(lambda e: e.tensor_copy(chd_s[0:NCHUNK, :],
                                       bank(7, 64)[0:NCHUNK, :]),
               deps=[e2])
    e4 = p.gdma(lambda e: e.dma_start(out=chdecT_d[:, :],
                                      in_=chd_s[0:NCHUNK, 0:NH]), deps=[e3])

    ar.pop()
    ar.push()
    zsc_s = ar.bf16(DIN // 2)
    zev_s = ar.bf16(512)
    xssd2 = ar.bf16(32 * 128 * NTB)
    btm2 = ar.bf16(128 * NTB)
    yn_tm = ar.bf16(NTB * DIN)
    bc_s = ar.bf16(2 * T1)
    y32_s = ar.f32(DIN)
    y1_s = ar.f32(1024)
    xdd2 = ar.bf16(NH * HD)
    ddf2 = ar.f32(CH)
    ddtm2 = ar.f32(64)
    lam2 = ar.f32(64)
    ebt2 = ar.f32(64)
    dec2 = ar.f32(64)
    wcol_s = ar.bf16(DIN)
    dcol_s = ar.f32(NH)
    mrow2_s = None
    TR = ar.f32(8704)
    assert ar.off <= ar.words, ar.off
    mrow2_s = y1_s[0:1, 960:1024]
    rsd_s = TR[:, 0:2048]
    r_sv = [TR[:, 2048:2560], TR[:, 4160:4672]]
    esc_sv = [TR[:, 2560:3072], TR[:, 4672:5184]]
    msc_sv = [TR[:, 3072:4096].bitcast(BF16),
              TR[:, 5184:6208].bitcast(BF16)]
    gm_s = TR[:, 4096:4160].bitcast(BF16)
    ln1f2 = TR[:, 0:4096].bitcast(BF16)
    wz_s = TR[:, 4096:8192].bitcast(BF16)
    xbcf2 = TR[:, 0:8704].bitcast(BF16)
    x2_s = TR[:, 0:2048]
    ao_s = TR[:, 2048:4096]
    wos_s = TR[:, 4096:8192].bitcast(BF16)

    wr0 = p.gdma(lambda e: e.dma_start(
        out=y32_s[0:1, 0:DIN],
        in_=AP(normw_d.ap().tensor, 0, [[DIN, 1], [1, DIN]])),
        deps=[e4] + tile_end_bar[-1])
    wtoks = []
    for j in range(8):
        wm = repmm(bank(j % 2), y32_s[0:1, j * 512:(j + 1) * 512],
                   [wr0] + wtoks[-1:])
        we = p.dve(lambda e, j=j: e.tensor_copy(
            wcol_s[:, j * 512:(j + 1) * 512], bank(j % 2)),
            deps=[wm] + wtoks[-1:])
        wtoks.extend([wm, we])
    t_wcol = wtoks[-1]
    dr0 = p.gdma(lambda e: e.dma_start(
        out=y32_s[0:1, 0:NH],
        in_=AP(dvec_d.ap().tensor, 0, [[NH, 1], [1, NH]])),
        deps=[e4, wtoks[-1]])
    dm = repmm(bank(2, 64), y32_s[0:1, 0:NH], [dr0])
    t_dcol = p.dve(lambda e: e.tensor_copy(dcol_s[:, :], bank(2, 64)),
                   deps=[dm])

    p2a_bar = [e4, t_wcol, t_dcol]
    for ti in range(NT1):
        tok0 = ti * T1
        bar0 = p2a_bar
        tl = p.gdma(lambda e, ti=ti: e.dma_start(
            out=ln1f2[:, :], in_=ln1x_sp[ti]), deps=bar0)
        z_done = []
        zsp = {tb: [] for tb in range(NTB)}
        wz_half = [wz_s[:, 0:4096], wz_s[:, 4096:8192]]
        wz_mm = [None, None]
        for zp in range(4):
            for piece in range(4):
                par = piece % 2

                def loadz(e, zp=zp, piece=piece, par=par):
                    a = winT_bf.ap()
                    src = AP(a.tensor,
                             a.offset + zp * 1024 + piece * 4 * 128 * PROJ,
                             [[PROJ, 128], [128 * PROJ, 4], [1, 1024]])
                    return e.dma_start(
                        out=expand(wz_half[par], 0, [(1024, 4), (1, 1024)]),
                        in_=src)
                wzt = p.gdma(loadz, deps=([wz_mm[par]] if wz_mm[par]
                                          else bar0) +
                             (t_cast_win if ti == 0 and zp == 0 else []))
                mm_toks = []
                for tb in range(NTB):
                    bk0 = tb * 2

                    def zmm(e, tb=tb, piece=piece, bk0=bk0, par=par):
                        r = None
                        for kk in range(4):
                            k = piece * 4 + kk
                            for zz in range(2):
                                r = e.matmul(
                                    bank(bk0 + zz),
                                    ln1f2[:, k * T1 + tb * 128:
                                          k * T1 + tb * 128 + 128],
                                    wz_half[par][:, kk * 1024 + zz * 512:
                                                 kk * 1024 + zz * 512
                                                 + 512],
                                    start=(k == 0), stop=(k == 15))
                        return r
                    mmt = p.pe(zmm, deps=[wzt, tl] + z_done[-2:])
                    mm_toks.append(mmt)
                    if piece == 3:
                        for zz in range(2):
                            sil = p.act(lambda e, bk0=bk0, zz=zz:
                                        e.activation(zev_s[:, :],
                                                     bank(bk0 + zz),
                                                     AF.Silu),
                                        deps=[mmt] + z_done[-1:])
                            sp = p.gdma(lambda e, ti=ti, tb=tb, zp=zp, zz=zz:
                                        e.dma_start(
                                            out=zsil_sp[ti, tb][
                                                :, zp * 1024 + zz * 512:
                                                zp * 1024 + zz * 512 + 512],
                                            in_=zev_s[:, :]), deps=[sil])
                            z_done.append(sp)
                            zsp[tb].append(sp)
                wz_mm[par] = mm_toks[-1]
        wz_tok = z_done[-1]
        t_xbc = p.gdma(lambda e, ti=ti: e.dma_start(
            out=expand(xbcf2, 0, [(T1, NSX), (1, T1)]),
            in_=xbc_sp[ti].rearrange("s q f -> q s f")),
            deps=z_done + [wz_tok, wz_mm[0], wz_mm[1]])
        tbc = p.dve(lambda e: e.tensor_copy(
            bc_s[:, :], xbcf2[:, 32 * T1:34 * T1]), deps=[t_xbc])
        inb = [xbcf2[:, s * T1 + tb * 128:s * T1 + tb * 128 + 128]
               for s in range(32) for tb in range(NTB)]
        outb = [xssd2[:, (s * NTB + tb) * 128:(s * NTB + tb) * 128 + 128]
                for s in range(32) for tb in range(NTB)]
        txp2 = pe_tx(outb, inb, [t_xbc] + bar0)
        inb = [xbcf2[:, 32 * T1 + tb * 128:32 * T1 + tb * 128 + 128]
               for tb in range(NTB)]
        outb = [btm2[:, tb * 128:(tb + 1) * 128] for tb in range(NTB)]
        tbt2_l = pe_tx(outb, inb, [t_xbc] + txp2[-1:])
        tbt2 = tbt2_l[-1]
        yn_toks = []
        for cc in range(NTB):
            tok_c = tok0 + cc * CH
            ci = ti * NTB + cc
            tzl = [p.gdma(lambda e, ti=ti, cc=cc, hh=hh: e.dma_start(
                out=zsc_s, in_=zsil_sp[ti, cc][:, hh * 2048:
                                               (hh + 1) * 2048]),
                deps=zsp[cc] + yn_toks[-1:] + ([None] if hh == 0 else []))
                for hh in range(1)]
            eb1 = p.act(lambda e, tok_c=tok_c: e.activation(
                ddf2[0:64, :], dt_v[:, tok_c:tok_c + CH], AF.Ln),
                deps=yn_toks[-1:] + [state_tok])
            eb2 = p.dve(lambda e, tok_c=tok_c: e.tensor_sub(
                ddf2[0:64, :], ddf2[0:64, :], acs_v[:, tok_c:tok_c + CH]),
                deps=[eb1])
            tb1 = p.pe(lambda e: e.transpose(
                bank(7, 64), ddf2[0:64, :], identf_s[0:64, 0:64]),
                deps=[eb2, t_idf] + yn_toks[-1:] + tbt2_l[-1:])
            tb2 = p.dve(lambda e: e.tensor_copy(ebt2[:, :], bank(7, 64)),
                        deps=[tb1])
            ta1 = p.pe(lambda e, tok_c=tok_c: e.transpose(
                bank(7, 64), acs_v[:, tok_c:tok_c + CH],
                identf_s[0:64, 0:64]),
                deps=[tb2, t_idf])
            ta2 = p.act(lambda e: e.activation(lam2[:, :], bank(7, 64),
                                               AF.Exp), deps=[ta1])
            dd1 = p.act(lambda e, tok_c=tok_c, ci=ci: e.activation(
                ddf2[64:128, :], acs_v[:, tok_c:tok_c + CH], AF.Exp,
                scale=-1.0, bias=acse_s[:, ci:ci + 1]), deps=[ta2, tb2])
            dd2 = p.dve(lambda e, tok_c=tok_c: e.tensor_mul(
                ddf2[64:128, :], ddf2[64:128, :], dt_v[:, tok_c:tok_c + CH]),
                deps=[dd1])
            dd3 = p.pe(lambda e: e.transpose(bank(7, 64), ddf2[64:128, :],
                                             identf_s[64:128, 64:128]),
                       deps=[dd2, t_idf, ta2])
            dd4 = p.dve(lambda e: e.tensor_copy(ddtm2[:, :], bank(7, 64)),
                        deps=[dd3])
            dcr = p.gdma(lambda e, ci=ci: e.dma_start(
                out=mrow2_s[0:1, 0:NH], in_=chdecT_d[ci:ci + 1, :]),
                deps=[e4] + bar0 + yn_toks[-1:] + [state_tok])
            dcm = repmm(bank(7, 64), mrow2_s[0:1, 0:NH], [dcr, dd4])
            dc1 = p.dve(lambda e: e.tensor_copy(dec2[:, :], bank(7, 64)),
                        deps=[dcm])
            x5 = p.dve(lambda e, cc=cc: e.tensor_tensor(
                expand(xdd2, 0, [(64, 64), (1, 64)]),
                expand(xssd2, cc * 128, [(NTB * 128, 32), (1, 128)]),
                expand(ddtm2, 0, [(2, 32), (1, 2), (0, 64)]), OP.mult),
                deps=[dd4] + txp2 + yn_toks[-1:])
            g1 = p.pe(lambda e, cc=cc: e.matmul(
                bank(6, 128),
                bc_s[:, cc * CH:(cc + 1) * CH],
                bc_s[:, T1 + cc * CH:T1 + (cc + 1) * CH],
                start=True, stop=True), deps=[tbc, dd3])
            g2 = p.dve(lambda e: e.tensor_tensor(gm_s[:, :], bank(6, 128),
                                                 tril_s[:, :], OP.mult),
                       deps=[g1, t_tril, tbc] + txp2[-1:])
            hg_toks = []
            rtoks = []
            prev_mm = {}
            for hg in range(4):
                r_s = r_sv[hg % 2]
                esc_s = esc_sv[hg % 2]
                msc_s = msc_sv[hg % 2]
                seed_toks = []
                for qd in range(4):
                    seed_toks.append(p.gdma(
                        lambda e, ci=ci, hg=hg, qd=qd: e.dma_start(
                            out=rsd_s[32 * qd:32 * qd + 1, :],
                            in_=AP(acs_sp.ap().tensor,
                                   (ci * NH + hg * 16) * CH,
                                   [[2048, 1], [1, 2048]])),
                        deps=[sp_acs_tok, tbc] + txp2[-1:] +
                        ([prev_mm[hg - 2]] if hg - 2 in prev_mm else []) +
                        rtoks[-4:]))
                m_toks = []
                for q in range(4):
                    h0 = hg * 16 + q * 4
                    rr = p.dve(lambda e, q=q, r_s=r_s: e.stream_shuffle(
                        r_s[:, :], rsd_s[:, q * 512:(q + 1) * 512],
                        [0] * 32), deps=seed_toks + m_toks[-1:])
                    rtoks.append(rr)
                    dsub = p.dve(lambda e, q=q, h0=h0, r_s=r_s,
                                 esc_s=esc_s: e.tensor_tensor(
                        expand(esc_s, 0, [(128, 4), (1, 128)]),
                        expand(r_s, 0, [(128, 4), (1, 128)]),
                        expand(ebt2, h0, [(1, 4), (0, 128)]),
                        OP.add), deps=[rtoks[-1], tb2] + m_toks[-1:])
                    dcl = p.dve(lambda e, esc_s=esc_s: e.tensor_scalar_min(
                        esc_s[:, :], esc_s[:, :], 20.0), deps=[dsub])
                    eex = p.act(lambda e, esc_s=esc_s: e.activation(
                        esc_s[:, :], esc_s[:, :], AF.Exp), deps=[dcl])
                    mml = p.dve(lambda e, q=q, esc_s=esc_s, msc_s=msc_s:
                                e.tensor_tensor(
                        expand(msc_s, q * 512, [(128, 4), (1, 128)]),
                        expand(esc_s, 0, [(128, 4), (1, 128)]),
                        expand(gm_s, 0, [(0, 4), (1, 128)]), OP.mult),
                        deps=[eex, g2])
                    m_toks.append(mml)
                pb = p.dve(lambda e, hg=hg: e.tensor_copy(
                    prevb_s[:, :], state_s[:, hg * 1024:(hg + 1) * 1024]),
                    deps=[state_tok] + hg_toks[-2:])

                def ydmm(e, hg=hg, cc=cc, msc_s=msc_s):
                    r = None
                    for hh in range(16):
                        h = hg * 16 + hh
                        off = ((h // 2) * NTB + cc) * 128 + (h % 2) * 64
                        r = e.matmul(
                            bank(hh // 8)[:, (hh % 8) * 64:
                                          (hh % 8) * 64 + 64],
                            msc_s[:, hh * 128:(hh + 1) * 128],
                            strided(xssd2, off, 1, 64),
                            start=(hh % 8 == 0), stop=(hh % 8 == 7))
                    return r
                tyd = p.pe(ydmm, deps=m_toks + txp2[-1:] + hg_toks[-2:])
                prev_mm[hg] = tyd

                def yomm(e, hg=hg, cc=cc):
                    r = None
                    for hh in range(16):
                        r = e.matmul(
                            bank(2 + hh // 8)[:, (hh % 8) * 64:
                                              (hh % 8) * 64 + 64],
                            bc_s[:, T1 + cc * CH:T1 + (cc + 1) * CH],
                            prevb_s[:, hh * 64:(hh + 1) * 64],
                            start=(hh % 8 == 0), stop=(hh % 8 == 7))
                    return r
                tyo = p.pe(yomm, deps=[pb, tbc] + hg_toks[-2:])

                def csm(e, hg=hg, cc=cc):
                    r = None
                    for hh in range(16):
                        h = hg * 16 + hh
                        r = e.matmul(
                            bank(4 + hh // 8)[:, (hh % 8) * 64:
                                              (hh % 8) * 64 + 64],
                            btm2[:, cc * 128:(cc + 1) * 128],
                            xdd2[:, h * 64:(h + 1) * 64],
                            start=(hh % 8 == 0), stop=(hh % 8 == 7))
                    return r
                tcs = p.pe(csm, deps=[x5] + tbt2_l + hg_toks[-2:])
                a1 = p.dve(lambda e, hg=hg: e.tensor_tensor(
                    y1_s[:, :], expand(pA, 1024, [(1, 1024)]),
                    expand(lam2, hg * 16, [(1, 16), (0, 64)]), OP.mult),
                    deps=[tyo, ta2])
                a2 = p.dve(lambda e, hg=hg: e.tensor_add(
                    y32_s[:, hg * 1024:(hg + 1) * 1024], y1_s[:, :],
                    expand(pA, 0, [(1, 1024)])), deps=[a1, tyd])
                a3 = p.dve(lambda e, hg=hg, cc=cc: e.tensor_tensor(
                    y1_s[:, :],
                    expand(xssd2, ((hg * 8) * NTB + cc) * 128,
                           [(NTB * 128, 8), (1, 128)]),
                    expand(dcol_s, hg * 16, [(2, 8), (1, 2), (0, 64)]),
                    OP.mult),
                    deps=[a2] + txp2)
                a4 = p.dve(lambda e, hg=hg: e.tensor_add(
                    y32_s[:, hg * 1024:(hg + 1) * 1024],
                    y32_s[:, hg * 1024:(hg + 1) * 1024], y1_s[:, :]),
                    deps=[a3])
                u1 = p.dve(lambda e, hg=hg: e.tensor_tensor(
                    state_s[:, hg * 1024:(hg + 1) * 1024],
                    state_s[:, hg * 1024:(hg + 1) * 1024],
                    expand(dec2, hg * 16, [(1, 16), (0, 64)]), OP.mult),
                    deps=[pb, dc1, state_tok, tyo])
                u2 = p.dve(lambda e, hg=hg: e.tensor_add(
                    state_s[:, hg * 1024:(hg + 1) * 1024],
                    state_s[:, hg * 1024:(hg + 1) * 1024],
                    expand(pA, 2048, [(1, 1024)])), deps=[u1, tcs])
                hg_toks.append(u2)
                hg_toks.append(a4)
            state_tok = hg_toks[-2]
            gm_t = tzl[0]
            gmuls = []
            for hh in range(2):
                gld = gm_t if hh == 0 else p.gdma(
                    lambda e, ti=ti, cc=cc: e.dma_start(
                        out=zsc_s, in_=zsil_sp[ti, cc][:, 2048:4096]),
                    deps=zsp[cc] + gmuls[-1:])
                gmt = p.dve(lambda e, hh=hh: e.tensor_tensor(
                    y32_s[:, hh * 2048:(hh + 1) * 2048],
                    y32_s[:, hh * 2048:(hh + 1) * 2048],
                    zsc_s[:, :], OP.mult),
                    deps=hg_toks + [gld])
                gmuls.append(gmt)
            gmul = gmuls[-1]
            sq = p.act(lambda e: e.activation(xdd2[:, :], y32_s[:, :],
                                              AF.Square),
                       deps=gmuls + cs_mm_toks[-1:])
            sq = p.dve(lambda e: e.tensor_reduce(
                tiny[:, 8:9], xdd2[:, :], mybir.AxisListType.X, OP.add),
                deps=[sq])
            r1 = p.act(lambda e: e.activation(tiny[:, 9:10], tiny[:, 8:9],
                                              AF.Ln, scale=1.0 / DIN,
                                              bias=cst_s[:, 0:1]),
                       deps=[sq] + t_cst)
            r2 = p.act(lambda e: e.activation(tiny[:, 9:10], tiny[:, 9:10],
                                              AF.Exp, scale=-0.5), deps=[r1])
            ynt = p.dve(lambda e, cc=cc: e.scalar_tensor_tensor(
                yn_tm[:, cc * DIN:(cc + 1) * DIN], y32_s[:, :],
                tiny[:, 9:10], wcol_s[:, :], OP.mult, OP.mult),
                deps=[r2, gmul, t_wcol])
            yn_toks.append(ynt)
        inb = [yn_tm[:, tb * DIN + s * 128:tb * DIN + s * 128 + 128]
               for s in range(32) for tb in range(NTB)]
        outb = [xssd2[:, (s * NTB + tb) * 128:(s * NTB + tb) * 128 + 128]
                for s in range(32) for tb in range(NTB)]
        tfy = pe_tx(outb, inb, yn_toks + txp2[-1:])
        wo_mm = [None, None]
        ao_toks = []
        xa_sp_toks = []
        wo_rel = []
        wos_half = [wos_s[:, 0:4096], wos_s[:, 4096:8192]]
        for tbp in range(NTB // 2):
            mm0f = mm1f = None
            for kg in range(16):
                par = kg % 2

                def loadwo(e, kg=kg, par=par):
                    a = woutT_bf.ap()
                    src = AP(a.tensor, a.offset + kg * 2 * 128 * HID,
                             [[HID, 128], [128 * HID, 2], [1, HID]])
                    return e.dma_start(
                        out=expand(wos_half[par], 0, [(HID, 2), (1, HID)]),
                        in_=src)
                wq = p.gdma if par == 0 else p.adma
                wot = wq(loadwo, deps=([wo_mm[par]] if wo_mm[par] else
                                       (yn_toks + [t_xbc])) +
                         (t_cast_wout if ti == 0 and tbp == 0 and kg < 2
                          else []))

                def womm(e, tb=0, kg=kg, par=par, bk0=0):
                    r = None
                    for kk in range(2):
                        k = kg * 2 + kk
                        for osp in range(4):
                            r = e.matmul(
                                bank(bk0 + osp),
                                xssd2[:, (k * NTB + tb) * 128:
                                      (k * NTB + tb) * 128 + 128],
                                wos_half[par][:, kk * HID + osp * 512:
                                              kk * HID + osp * 512 + 512],
                                start=(k == 0), stop=(k == 31))
                    return r
                mm0f = p.pe(lambda e, f=womm, tbp=tbp: f(e, tb=2 * tbp,
                                                         bk0=0),
                            deps=[wot] + tfy[-1:] +
                            (wo_rel[-2:-1] if kg == 0 else []))
                mm1f = p.pe(lambda e, f=womm, tbp=tbp: f(e, tb=2 * tbp + 1,
                                                         bk0=4),
                            deps=[wot] + (wo_rel[-1:] if kg == 0 else []))
                wo_mm[par] = mm1f
            for sub in range(2):
                tb = 2 * tbp + sub
                mmf = mm0f if sub == 0 else mm1f
                st3 = ln_stats(expand(pA, sub * 2048, [(1, 2048)]), tb,
                               [mmf])
                t_x2 = p.gdma(lambda e, ti=ti, tb=tb: e.dma_start(
                    out=x2_s, in_=dram_tb(x_d, ti * NTB + tb, HID)),
                    deps=(yn_toks if ti == 0 and tb == 0 else []) +
                    ao_toks[-1:])
                xa1 = p.dve(lambda e, tb=tb, sub=sub: e.scalar_tensor_tensor(
                    ao_s[:, :], expand(pA, sub * 2048, [(1, 2048)]),
                    rs_s[:, tb, 0:1], x2_s[:, :], OP.mult, OP.add),
                    deps=[st3, t_x2] + xa_sp_toks[-1:])
                xa2 = p.dve(lambda e, tb=tb: e.tensor_scalar(
                    ao_s[:, :], ao_s[:, :], mrs_s[:, tb, 0:1], None,
                    OP.subtract), deps=[xa1])
                wo_rel.append(xa1)
                ao_toks.append(xa2)
                t_xasp = p.gdma(lambda e, ti=ti, tb=tb: e.dma_start(
                    out=dram_tb(xa_sp, ti * NTB + tb, HID),
                    in_=ao_s), deps=[xa2])
                xa_sp_toks.append(t_xasp)
        if qstate["inv"] is not None:
            emit_quant(30)
        p2a_bar = xa_sp_toks + [state_tok, wo_mm[0], wo_mm[1]] + \
            yn_toks[-1:] + tfy[-1:] + wo_rel[-2:] + ao_toks[-1:]

    while qstate["qi"] < len(quant_jobs):
        emit_quant(16)
    qdone = qstate["qtoks"][-Prog.NQS:]

    ar.pop()
    ar.push()
    hf_s = ar.bf16(64 * 128 * HTB)
    ln3f_s = ar.bf16(16 * T2)
    T2R = ar.f32(8192)
    xa2_s = ar.f32(HID)
    bo_s = ar.f32(HID)
    out_s = ar.f32(HID)
    assert ar.off <= ar.words, ar.off
    ln3_tm = T2R[:, 0:NTB2 * HID // 2].bitcast(BF16)
    w1b_s = [T2R[:, 2048 * i:2048 * (i + 1)].bitcast(BF16) for i in range(2)]
    w2b_s = [T2R[:, 4096 + 1024 * i:4096 + 1024 * (i + 1)].bitcast(BF16)
             for i in range(2)]
    b2col_s = T2R[:, 6144:8192]

    p2b_bar = p2a_bar + qdone
    prev_w1mm = []
    w2war = [None, None]
    last_pe1 = []
    prev_o1 = []
    prev_tso = []
    for tj in range(NT2):
        tok0 = tj * T2
        bar0 = p2b_bar
        ln3_toks = []
        xa_tok = None
        for tb in range(NTB2):
            tx = p.gdma(lambda e, tok0=tok0, tb=tb: e.dma_start(
                out=xa2_s, in_=dram_tb(xa_sp, tok0 // 128 + tb, HID)),
                deps=bar0 + ([xa_tok] if xa_tok else []) + ln3_toks[-1:])
            st = ln_stats(xa2_s, tb, [tx])
            ta = p.dve(lambda e, tb=tb: e.tensor_scalar(
                ln3_tm[:, tb * HID:(tb + 1) * HID], xa2_s[:, :],
                stat2[:, tb, 0:1], rs_s[:, tb, 0:1], OP.subtract, OP.mult),
                deps=[st])
            ln3_toks.append(ta)
            xa_tok = tx
        inb = [ln3_tm[:, tb * HID + s * 128:tb * HID + s * 128 + 128]
               for s in range(16) for tb in range(NTB2)]
        outb = [ln3f_s[:, (s * NTB2 + tb) * 128:(s * NTB2 + tb) * 128 + 128]
                for s in range(16) for tb in range(NTB2)]
        tf3 = pe_tx(outb, inb, ln3_toks + bar0)
        b2r = p.gdma(lambda e: e.dma_start(
            out=xa2_s[0:1, :],
            in_=AP(b2_d.ap().tensor, 0, [[HID, 1], [1, HID]])),
            deps=[ln3_toks[-1], xa_tok])
        b2toks = []
        for j in range(4):
            bm = repmm(bank(6), xa2_s[0:1, j * 512:(j + 1) * 512],
                       [b2r] + b2toks[-1:] + tf3)
            be = p.dve(lambda e, j=j: e.tensor_copy(
                b2col_s[:, j * 512:(j + 1) * 512], bank(6)),
                deps=[bm] + b2toks[-1:] + ln3_toks)
            b2toks.extend([bm, be])
        t_b2col = b2toks[-1]
        for half in range(2):
            h_toks = []
            w1mm = []
            for fp in range(32):
                def loadw1(e, fp=fp):
                    a = w1qB_bf.ap()
                    src = AP(a.tensor, a.offset + fp * 128 * 4096,
                             [[4096, 128], [1, 4096]])
                    return e.dma_start(out=w1b_s[fp % 2][:, :], in_=src)
                war = w1mm[-3:-2] if fp >= 2 else prev_w1mm[-1:]
                w1t = (p.gdma if fp % 2 == 0 else p.adma)(
                    loadw1, deps=war +
                    (bar0 + ln3_toks + tf3 + qdone if fp < 2 else []))
                for fi in range(2):
                    fs = fp * 2 + fi
                    bk = fs % 4

                    def f1mm(e, fp=fp, fi=fi, half=half, bk=bk):
                        r = None
                        for k in range(16):
                            r = e.matmul(
                                bank(bk, 128 * HTB),
                                w1b_s[fp % 2][:, k * 256 + fi * 128:
                                              k * 256 + fi * 128 + 128],
                                expand(ln3f_s,
                                       (k * NTB2 + half * HTB) * 128,
                                       [(128, HTB), (1, 128)]),
                                start=(k == 0), stop=(k == 15))
                        return r
                    mmt = p.pe(f1mm, deps=[w1t] + tf3 + h_toks[-4:-3] +
                               (last_pe1[:1] if fs == 0 else []))
                    w1mm.append(mmt)
                    gl = p.act(lambda e, fs=fs, bk=bk: e.activation(
                        hf_s[:, fs * 128 * HTB:(fs + 1) * 128 * HTB],
                        bank(bk, 128 * HTB),
                        AF.Gelu, bias=b1_s[:, fs:fs + 1]),
                        deps=[mmt, t_b1])
                    h_toks.append(gl)
            prev_w1mm = w1mm
            for tbp in range(HTB // 2):
                mm0 = mm1 = None
                for k in range(64):
                    par = k % 2

                    def loadw2(e, k=k, par=par):
                        return e.dma_start(
                            out=w2b_s[par][:, :],
                            in_=w2qT_bf[k * 128:(k + 1) * 128, :])
                    w2t = (p.gdma if par == 0 else p.adma)(
                        loadw2, deps=([w2war[par]] if w2war[par] else []) +
                        ([t_b2col] + tf3 + qdone if k < 2 else []))

                    def f2mm(e, k=k, par=par, tb=0, bk0=0):
                        r = None
                        for osp in range(4):
                            r = e.matmul(
                                bank(bk0 + osp),
                                hf_s[:, k * 128 * HTB + tb * 128:
                                     k * 128 * HTB + tb * 128 + 128],
                                w2b_s[par][:, osp * 512:osp * 512 + 512],
                                start=(k == 0), stop=(k == 63))
                        return r
                    mm0 = p.pe(lambda e, f=f2mm, tbp=tbp:
                               f(e, tb=2 * tbp, bk0=0),
                               deps=[w2t] +
                               ([h_toks[-1]] + last_pe1[:1]
                                if k == 0 else []))
                    mm1 = p.pe(lambda e, f=f2mm, tbp=tbp:
                               f(e, tb=2 * tbp + 1, bk0=4),
                               deps=[w2t] +
                               ([t_b2col] + tf3[-1:] + last_pe1[-1:]
                                if k == 0 else []))
                    w2war[par] = mm1
                new_pe1 = []
                for sub in range(2):
                    gtb = half * HTB + 2 * tbp + sub
                    mmf = mm0 if sub == 0 else mm1
                    pe1 = p.dve(lambda e, sub=sub: e.tensor_add(
                        bo_s[:, :], expand(pA, sub * 2048, [(1, 2048)]),
                        b2col_s[:, :]), deps=[mmf, t_b2col])
                    st3 = ln_stats(bo_s[:, :], 4 + sub, [pe1])
                    txa = p.gdma(lambda e, tok0=tok0, gtb=gtb: e.dma_start(
                        out=xa2_s, in_=dram_tb(xa_sp, tok0 // 128 + gtb,
                                               HID)),
                        deps=[pe1] + prev_o1 +
                        (b2toks[-2:] + ln3_toks[-1:]
                         if half == 0 and tbp == 0 and sub == 0 else []))
                    o1 = p.dve(lambda e, sub=sub: e.scalar_tensor_tensor(
                        out_s[:, :], bo_s[:, :], rs_s[:, 4 + sub, 0:1],
                        xa2_s[:, :], OP.mult, OP.add),
                        deps=[st3, txa] + prev_tso)
                    o2 = p.dve(lambda e, sub=sub: e.tensor_scalar(
                        out_s[:, :], out_s[:, :], mrs_s[:, 4 + sub, 0:1],
                        None, OP.subtract), deps=[o1])
                    tso = p.gdma(lambda e, tok0=tok0, gtb=gtb: e.dma_start(
                        out=dram_tb(out_d, tok0 // 128 + gtb, HID),
                        in_=out_s), deps=[o2])
                    prev_o1 = [o1]
                    prev_tso = [tso]
                    new_pe1.append(pe1)
                last_pe1 = new_pe1
        p2b_bar = prev_tso + [w2war[0], w2war[1], h_toks[-1]] + last_pe1
    ar.pop()

    if dbg:
        fin = p.bar()
        for nm, t, shp in (
                ("dbg_ln1", ln1x_sp, [NT1 * 128 * 16 * T1]),
                ("dbg_xbc", xbc_sp, [NT1 * NSX * 128 * T1]),
                ("dbg_acs", acs_sp, [NCHUNK * NH * CH]),
                ("dbg_zsil", zsil_sp, [NT1 * NTB * 128 * DIN]),
                ("dbg_sin", s_in_d, [128 * NH * HD]),
                ("dbg_sout", s_out_d, [128 * NH * HD]),
                ("dbg_tdec", tdec_d, [NT1 * NH]),
        ):
            dt_ = t.dtype if hasattr(t, 'dtype') else F32
            o = nc.dram_tensor(nm, shp, dt_, kind="ExternalOutput")
            a = t.ap()
            flat = AP(a.tensor, 0, [[0, 1], [1, shp[0]]])
            p.gdma(lambda e, o=o, flat=flat, n=shp[0]: e.dma_start(
                out=AP(o.ap().tensor, 0, [[0, 1], [1, n]]), in_=flat),
                deps=fin)
        for nm, t in (
                ("dbg_xssd", xssd_tm),
                ("dbg_btm", btm_s),
                ("dbg_xdd", xdd_tm),
        ):
            w = t.shape[-1]
            o = nc.dram_tensor(nm, [128, w], BF16, kind="ExternalOutput")
            p.gdma(lambda e, o=o, t=t: e.dma_start(out=o[:, :], in_=t),
                   deps=fin)
        for nm, t, shp in (
                ("dbg_xa", xa_sp, [TOK * HID]),
                ("dbg_w1q", w1qB_bf, [HID * 8192]),
                ("dbg_chd", chdecT_d, [NCHUNK * NH]),
        ):
            dt_ = t.dtype if hasattr(t, 'dtype') else F32
            o = nc.dram_tensor(nm, shp, dt_, kind="ExternalOutput")
            a = t.ap()
            flat = AP(a.tensor, 0, [[0, 1], [1, shp[0]]])
            p.gdma(lambda e, o=o, flat=flat, n=shp[0]: e.dma_start(
                out=AP(o.ap().tensor, 0, [[0, 1], [1, n]]), in_=flat),
                deps=fin)

    p.build(exits)
    return nc, dict(TOK=TOK, exits=exits)


def make_inmaps(inputs, TOK=4096):
    x = np.ascontiguousarray(np.asarray(inputs["L0_attn_hidden_states"],
                                        np.float32))
    B, L, _ = x.shape
    winT = np.ascontiguousarray(np.asarray(inputs["L0_attn_in_proj_w"],
                                           np.float32).T)
    woutT = np.ascontiguousarray(np.asarray(inputs["L0_attn_out_proj_w"],
                                            np.float32).T)
    w1T = np.ascontiguousarray(np.asarray(inputs["L0_ffn_w1"], np.float32).T)
    w2T = np.ascontiguousarray(np.asarray(inputs["L0_ffn_w2"], np.float32).T)
    sl = np.arange(128)
    tril = (sl[None, :] >= sl[:, None]).astype(np.float32)
    ident = np.eye(128, dtype=np.float32)
    shared = dict(
        winT=winT, woutT=woutT, w1T=w1T, w2T=w2T,
        conv_w=np.ascontiguousarray(np.asarray(inputs["L0_attn_conv_w"],
                                               np.float32)),
        conv_b=np.ascontiguousarray(np.asarray(inputs["L0_attn_conv_b"],
                                               np.float32)),
        dt_bias=np.ascontiguousarray(np.asarray(inputs["L0_attn_dt_bias"],
                                                np.float32)),
        A_log=np.ascontiguousarray(np.asarray(inputs["L0_attn_A_log"],
                                              np.float32)),
        Dvec=np.ascontiguousarray(np.asarray(inputs["L0_attn_D"],
                                             np.float32)),
        norm_w=np.ascontiguousarray(np.asarray(inputs["L0_attn_norm_w"],
                                               np.float32)),
        b1=np.ascontiguousarray(np.asarray(inputs["L0_ffn_b1"], np.float32)),
        b2=np.ascontiguousarray(np.asarray(inputs["L0_ffn_b2"], np.float32)),
        tril=tril, identf=ident)
    in_maps = []
    ncore = (B * L) // TOK
    for core in range(ncore):
        per_batch = L // TOK
        b = core // per_batch
        h = core % per_batch
        t0 = h * TOK
        xs = np.ascontiguousarray(x[b, t0:t0 + TOK])
        halo = (np.zeros((3, HID), np.float32) if h == 0
                else np.ascontiguousarray(x[b, t0 - 3:t0]))
        m = dict(shared)
        m.update(x=xs, xT=np.ascontiguousarray(xs.T), xhalo=halo,
                 cmask=np.full((1, 1),
                               1.0 if h < per_batch - 1 else 0.0, np.float32),
                 imask=np.full((1, 1), 1.0 if h > 0 else 0.0, np.float32))
        in_maps.append(m)
    return in_maps


def kernel(**inputs):
    TOK = 4096
    nc, meta = build_program(TOK=TOK)
    in_maps = make_inmaps(inputs, TOK)
    res = run_bass_kernel_spmd(nc, in_maps, core_ids=list(range(NCORE)))
    meta["exits"].close()
    outs = [np.asarray(r["out"]) for r in res.results]
    x = inputs["L0_attn_hidden_states"]
    B, L, _ = np.asarray(x).shape
    full = np.zeros((B, L, HID), np.float32)
    for core in range(NCORE):
        b, h = core // 2, core % 2
        full[b, h * TOK:(h + 1) * TOK] = outs[core]
    return full
